# version 02
# baseline (speedup 1.0000x reference)
import numpy as np
import ml_dtypes
import concourse.bass as bass
import concourse.mybir as mybir
from concourse.bass_utils import run_bass_kernel_spmd

F32 = mybir.dt.float32
BF16 = mybir.dt.bfloat16
AF = mybir.ActivationFunctionType
ALU = mybir.AluOpType
AX = mybir.AxisListType
BF = ml_dtypes.bfloat16

NEG_BIG = -1.0e30
INV_SQRT_DK = 0.08838834764831845


class Prog:

    ENGINES = ("y", "g", "v", "s", "t")

    def __init__(self):
        self.ops = []
        self.writers = {}
        self.readers = {}

    def op(self, eng, fn, reads=(), writes=(), dma=False, accum=False, tag=None):
        i = len(self.ops)
        deps = set()
        for k in reads:
            deps.update(self.writers.get(k, ()))
        for k in writes:
            deps.update(self.writers.get(k, ()))
            deps.update(self.readers.get(k, ()))
        for k in reads:
            self.readers.setdefault(k, []).append(i)
        for k in writes:
            if accum:
                self.writers.setdefault(k, []).append(i)
            else:
                self.writers[k] = [i]
                self.readers[k] = []
        deps.discard(i)
        self.ops.append({"eng": eng, "fn": fn, "deps": deps, "dma": dma, "i": i,
                         "tag": tag})
        return i

    def emit(self, nc, block, sems):
        n = len(self.ops)
        need = [False] * n
        for o in self.ops:
            if o["dma"]:
                need[o["i"]] = True
            for d in o["deps"]:
                need[d] = True
        cum = {}
        dma_idx = {}
        for o in self.ops:
            if o["dma"]:
                sk = ("d", o["eng"], o["tag"] or "d")
                dma_idx.setdefault(sk, []).append(o["i"])
            else:
                sk = ("c", o["eng"])
            o["sk"] = sk
            if need[o["i"]]:
                cum[sk] = cum.get(sk, 0) + (16 if o["dma"] else 1)
            o["val"] = cum.get(sk, 0)
            o["inc"] = need[o["i"]]
        self.final = dict(cum)
        import bisect
        self.dma_idx = dma_idx
        self.bisect = bisect

        per_eng = {e: [] for e in self.ENGINES}
        for o in self.ops:
            per_eng[o["eng"]].append(o)
        hw = {}

        def runner(ops, eng_key):
            def _run(eng):
                for o in ops:
                    for d in sorted(o["deps"]):
                        p = self.ops[d]
                        sk = p["sk"]
                        if p["dma"]:
                            v = 16 * self.bisect.bisect_left(self.dma_idx[sk], o["i"])
                        else:
                            v = p["val"]
                        if hw.get((eng_key, sk), 0) < v:
                            eng.wait_ge(sems[sk], v)
                            hw[(eng_key, sk)] = v
                    ins = o["fn"](eng)
                    if o["inc"]:
                        ins.then_inc(sems[o["sk"]], 16 if o["dma"] else 1)
                if eng_key == "y":
                    for sk, v in self.final.items():
                        if sk[0] == "d" and v > 0 and hw.get((eng_key, sk), 0) < v:
                            eng.wait_ge(sems[sk], v)
            return _run

        for e in self.ENGINES:
            sec = {"y": block.sync, "g": block.gpsimd, "v": block.vector,
                   "s": block.scalar, "t": block.tensor}[e]
            sec(runner(per_eng[e], e))

    def used_streams(self):
        s = set()
        for o in self.ops:
            if o["dma"]:
                s.add(("d", o["eng"], o["tag"] or "d"))
            else:
                s.add(("c", o["eng"]))
        return s


def _alloc(es, nc, name, shape, dt):
    return es.enter_context(nc.sbuf_tensor(name, shape, dt))


def build_phase1(T=8192):
    nc = bass.Bass("TRN2", target_bir_lowering=False, debug=False)
    NST = T // 512
    x_d = nc.dram_tensor("x", [T, 1024], F32, kind="ExternalInput").ap()
    w_d = nc.dram_tensor("wqkvg", [1024, 1540], F32, kind="ExternalInput").ap()
    bg_d = nc.dram_tensor("bg", [4, 1], F32, kind="ExternalInput").ap()
    gain_d = nc.dram_tensor("gain", [128, 1], F32, kind="ExternalInput").ap()
    y_d = nc.dram_tensor("y", [4, 128, T], BF16, kind="ExternalOutput").ap()
    gsc_d = nc.dram_tensor("gsc", [4, 8], BF16, kind="Internal").ap()

    import contextlib
    es = contextlib.ExitStack()
    with es:
        w_sb = _alloc(es, nc, "w_sb", [128, 8, 1540], BF16)
        x_sb = [_alloc(es, nc, f"x_sb{i}", [128, 4, 1024], F32) for i in range(2)]
        xnf = _alloc(es, nc, "xnf", [128, 8, 512], BF16)
        xn = [_alloc(es, nc, f"xn{i}", [128, 1024], BF16) for i in range(2)]
        scr = _alloc(es, nc, "scr", [128, 1024], F32)
        qf = _alloc(es, nc, "qf", [128, 512], F32)
        kf = _alloc(es, nc, "kf", [128, 512], F32)
        qn = [_alloc(es, nc, f"qn{i}", [128, 512], BF16) for i in range(2)]
        kn = [_alloc(es, nc, f"kn{i}", [128, 512], BF16) for i in range(4)]
        v_sb = [_alloc(es, nc, f"v_sb{i}", [128, 4, 512], BF16) for i in range(2)]
        qkT = [_alloc(es, nc, f"qkT{i}", [128, 1024], BF16) for i in range(4)]
        attnT = [_alloc(es, nc, f"attnT{i}", [128, 512], BF16) for i in range(4)]
        smask = _alloc(es, nc, "smask", [128, 512], F32)
        pall = _alloc(es, nc, "pall", [128, 512], F32)
        attnf = _alloc(es, nc, "attnf", [128, 512], F32)
        y_sb = [_alloc(es, nc, f"y_sb{i}", [128, 512], BF16) for i in range(2)]
        S_f = _alloc(es, nc, "S_f", [128, 512], F32)
        S_b = _alloc(es, nc, "S_b", [128, 512], BF16)
        st1 = _alloc(es, nc, "st1", [128, 16], F32)
        qs4 = _alloc(es, nc, "qs4", [128, 16], F32)
        ks4 = _alloc(es, nc, "ks4", [128, 16], F32)
        negmax = _alloc(es, nc, "negmax", [128, 4], F32)
        rowsum = _alloc(es, nc, "rowsum", [128, 4], F32)
        rsum = _alloc(es, nc, "rsum", [128, 4], F32)
        gbc = _alloc(es, nc, "gbc", [128, 32], F32)
        ge = _alloc(es, nc, "ge", [4, 512], F32)
        glp = _alloc(es, nc, "glp", [4, 512], F32)
        lng = _alloc(es, nc, "lng", [4, 8], F32)
        gam = _alloc(es, nc, "gam", [4, 8], F32)
        gambf = _alloc(es, nc, "gambf", [4, 8], BF16)
        gamrow = _alloc(es, nc, "gamrow", [1, 32], BF16)
        bg_sb = _alloc(es, nc, "bg_sb", [4, 1], F32)
        nbg = _alloc(es, nc, "nbg", [4, 1], F32)
        gain_sb = _alloc(es, nc, "gain_sb", [128, 1], F32)
        ones1 = _alloc(es, nc, "ones1", [128, 1], F32)
        eps_ln = _alloc(es, nc, "eps_ln", [128, 1], F32)
        eps_rms = _alloc(es, nc, "eps_rms", [128, 1], F32)
        onesrow = _alloc(es, nc, "onesrow", [1, 128], BF16)
        onesbf = _alloc(es, nc, "onesbf", [128, 128], BF16)
        onesf = _alloc(es, nc, "onesf", [128, 128], F32)
        eyeb = _alloc(es, nc, "eyeb", [128, 128], BF16)
        eyef = _alloc(es, nc, "eyef", [128, 128], F32)
        mask = _alloc(es, nc, "mask", [128, 128], F32)

        PxnT = es.enter_context(nc.psum_tensor("PxnT", [128, 1024], BF16))
        Pqkv = es.enter_context(nc.psum_tensor("Pqkv", [128, 1536], F32))
        PqkT = es.enter_context(nc.psum_tensor("PqkT", [128, 1024], BF16))
        Psc = es.enter_context(nc.psum_tensor("Psc", [128, 512], F32))
        Py = es.enter_context(nc.psum_tensor("Py", [128, 512], F32))
        Pg = es.enter_context(nc.psum_tensor("Pg", [128, 512], F32))

        p = Prog()
        O = p.op

        O("g", lambda g: g.dma_start(
            out=w_sb[:], in_=w_d.rearrange("(dc p) f -> p dc f", p=128)),
            writes=["wall"], dma=True, tag="ini")
        O("y", lambda e: e.dma_start(out=bg_sb[:], in_=bg_d[:]), writes=["bg"], dma=True, tag="ini")
        O("y", lambda e: e.dma_start(out=gain_sb[:], in_=gain_d[:]), writes=["gain"], dma=True, tag="ini")
        O("v", lambda v: v.memset(ones1[:], 1.0), writes=["ones1"])
        O("v", lambda v: v.memset(eps_ln[:], 1e-5), writes=["eps_ln"])
        O("v", lambda v: v.memset(eps_rms[:], 1e-6), writes=["eps_rms"])
        O("v", lambda v: v.memset(onesrow[:], 1.0), writes=["onesrow"])
        O("v", lambda v: v.memset(onesbf[:], 1.0), writes=["onesbf"])
        O("v", lambda v: v.memset(onesf[:], 1.0), writes=["onesf"])
        O("v", lambda v: v.memset(mask[:], 0.0), writes=["mask"])
        O("v", lambda v: v.memset(S_f[:], 0.0), writes=["Sf0", "Sf1", "Sf2", "Sf3"])
        O("v", lambda v: v.memset(S_b[:], 0.0), writes=["Sb0", "Sb1", "Sb2", "Sb3"])
        O("v", lambda v: v.tensor_scalar_mul(nbg[:], bg_sb[:], -1.0),
          reads=["bg"], writes=["nbg"])
        O("g", lambda g: g.affine_select(eyeb[:], onesbf[:], pattern=[[-1, 128]],
                                         compare_op=ALU.is_equal, fill=0.0,
                                         base=0, channel_multiplier=1),
          reads=["onesbf"], writes=["eyeb"])
        O("g", lambda g: g.affine_select(eyef[:], onesf[:], pattern=[[-1, 128]],
                                         compare_op=ALU.is_equal, fill=0.0,
                                         base=0, channel_multiplier=1),
          reads=["onesf"], writes=["eyef"])
        O("g", lambda g: g.affine_select(mask[:], mask[:], pattern=[[-1, 128]],
                                         compare_op=ALU.is_ge, fill=NEG_BIG,
                                         base=0, channel_multiplier=1),
          reads=["mask"], writes=["mask"])
        O("g", lambda g: g.affine_select(mask[64:128, :], mask[64:128, :],
                                         pattern=[[1, 128]],
                                         compare_op=ALU.is_ge, fill=NEG_BIG,
                                         base=-64, channel_multiplier=0),
          reads=["mask"], writes=["mask"])
        mask_bc = mask[:].unsqueeze(1).broadcast_to((128, 4, 128))

        for st in range(NST):
            sl = st % 2
            T0 = st * 512
            O("y", lambda e, T0=T0, sl=sl: e.dma_start(
                out=x_sb[sl][:], in_=x_d[T0:T0 + 512].rearrange("(tt p) d -> p tt d", p=128)),
                writes=[f"x{sl}"], dma=True, tag=f"x{sl}")

            for tt in range(4):
                par = tt % 2
                xin = x_sb[sl][:, tt, :]
                O("v", lambda v, xin=xin: v.reduce_sum(st1[:, 0:1], xin, axis=AX.X),
                  reads=[f"x{sl}"], writes=["st1"])
                O("s", lambda s, xin=xin: s.activation(scr[:], xin, AF.Square,
                                                       accum_out=st1[:, 1:2]),
                  reads=[f"x{sl}"], writes=["scr", "st1b"])
                O("v", lambda v: v.tensor_scalar_mul(st1[:, 2:3], st1[:, 0:1], 1.0 / 1024),
                  reads=["st1"], writes=["mean"])
                O("v", lambda v: v.tensor_scalar_mul(st1[:, 3:4], st1[:, 1:2], 1.0 / 1024),
                  reads=["st1b"], writes=["ms"])
                O("v", lambda v: v.tensor_tensor(st1[:, 4:5], st1[:, 2:3], st1[:, 2:3], ALU.mult),
                  reads=["mean"], writes=["m2"])
                O("v", lambda v: v.tensor_sub(st1[:, 5:6], st1[:, 3:4], st1[:, 4:5]),
                  reads=["ms", "m2"], writes=["var"])
                O("s", lambda s: s.activation(st1[:, 6:7], st1[:, 5:6], AF.Ln, bias=eps_ln[:]),
                  reads=["var", "eps_ln"], writes=["lnv"])
                O("s", lambda s: s.activation(st1[:, 7:8], st1[:, 6:7], AF.Exp, scale=-0.5),
                  reads=["lnv"], writes=["inv"])
                O("v", lambda v: v.scalar_tensor_tensor(st1[:, 8:9], st1[:, 2:3], -1.0,
                                                        st1[:, 7:8], ALU.mult, ALU.mult),
                  reads=["mean", "inv"], writes=["nmi"])
                O("s", lambda s, xin=xin, par=par: s.activation(
                    xn[par][:], xin, AF.Identity, scale=st1[:, 7:8], bias=st1[:, 8:9]),
                  reads=[f"x{sl}", "inv", "nmi"], writes=[f"xn{par}"])

                def xnT_fn(t, par=par):
                    for k in range(8):
                        ins = t.transpose(PxnT[:, k * 128:(k + 1) * 128],
                                          xn[par][:, k * 128:(k + 1) * 128], eyeb[:])
                    return ins
                O("t", xnT_fn, reads=[f"xn{par}", "eyeb"], writes=["PxnT"])
                O("v", lambda v, tt=tt: v.tensor_copy(
                    xnf[:, :, tt * 128:(tt + 1) * 128],
                    PxnT[:].rearrange("p (dc t) -> p dc t", t=128)),
                  reads=["PxnT"], writes=[f"xnf{tt}"])

                for j, (c0, pk) in enumerate(((0, "Pq"), (512, "Pk"), (1024, "Pv"))):
                    def prj(t, tt=tt, c0=c0):
                        for dc in range(8):
                            ins = t.matmul(Pqkv[:, c0:c0 + 512],
                                           xnf[:, dc, tt * 128:(tt + 1) * 128],
                                           w_sb[:, dc, c0:c0 + 512],
                                           start=(dc == 0), stop=(dc == 7))
                        return ins
                    O("t", prj, reads=[f"xnf{tt}", "wall"], writes=[pk])
                O("s", lambda s: s.copy(qf[:], Pqkv[:, 0:512]), reads=["Pq"], writes=["qfb"])
                O("s", lambda s: s.copy(kf[:], Pqkv[:, 512:1024]), reads=["Pk"], writes=["kfb"])
                O("s", lambda s, sl=sl, tt=tt: s.copy(v_sb[sl][:, tt, :], Pqkv[:, 1024:1536]),
                  reads=["Pv"], writes=[f"v{sl}_{tt}"])

                O("s", lambda s: s.activation(scr[:, 0:512], qf[:], AF.Square),
                  reads=["qfb"], writes=["scr"])
                O("v", lambda v: v.reduce_sum(
                    qs4[:, 0:4], scr[:, 0:512].rearrange("p (h e) -> p h e", h=4), axis=AX.X),
                  reads=["scr"], writes=["qss"])
                O("s", lambda s: s.activation(qs4[:, 4:8], qs4[:, 0:4], AF.Ln,
                                              scale=1.0 / 128, bias=eps_rms[:]),
                  reads=["qss", "eps_rms"], writes=["lnq"])
                O("s", lambda s: s.activation(qs4[:, 8:12], qs4[:, 4:8], AF.Exp, scale=-0.5),
                  reads=["lnq"], writes=["qrs"])
                O("v", lambda v: v.tensor_scalar(qs4[:, 12:16], qs4[:, 8:12],
                                                 gain_sb[:], None, ALU.mult),
                  reads=["qrs", "gain"], writes=["qsc"])
                O("v", lambda v, par=par: v.tensor_tensor(
                    qn[par][:].rearrange("p (h e) -> p h e", h=4),
                    qf[:].rearrange("p (h e) -> p h e", h=4),
                    qs4[:, 12:16].unsqueeze(2).broadcast_to((128, 4, 128)), ALU.mult),
                  reads=["qfb", "qsc"], writes=[f"qn{par}"])
                O("s", lambda s: s.activation(scr[:, 0:512], kf[:], AF.Square),
                  reads=["kfb"], writes=["scr"])
                O("v", lambda v: v.reduce_sum(
                    ks4[:, 0:4], scr[:, 0:512].rearrange("p (h e) -> p h e", h=4), axis=AX.X),
                  reads=["scr"], writes=["kss"])
                O("s", lambda s: s.activation(ks4[:, 4:8], ks4[:, 0:4], AF.Ln,
                                              scale=1.0 / 128, bias=eps_rms[:]),
                  reads=["kss", "eps_rms"], writes=["lnk"])
                O("s", lambda s: s.activation(ks4[:, 8:12], ks4[:, 4:8], AF.Exp, scale=-0.5),
                  reads=["lnk"], writes=["krs"])
                O("v", lambda v, tt=tt: v.tensor_tensor(
                    kn[tt][:].rearrange("p (h e) -> p h e", h=4),
                    kf[:].rearrange("p (h e) -> p h e", h=4),
                    ks4[:, 8:12].unsqueeze(2).broadcast_to((128, 4, 128)), ALU.mult),
                  reads=["kfb", "krs"], writes=[f"kn{tt}"])

                def qkT_fn(t, par=par, tt=tt):
                    for h in range(4):
                        t.transpose(PqkT[:, h * 128:(h + 1) * 128],
                                    qn[par][:, h * 128:(h + 1) * 128], eyeb[:])
                    for h in range(4):
                        ins = t.transpose(PqkT[:, 512 + h * 128:512 + (h + 1) * 128],
                                          kn[tt][:, h * 128:(h + 1) * 128], eyeb[:])
                    return ins
                O("t", qkT_fn, reads=[f"qn{par}", f"kn{tt}", "eyeb"], writes=["PqkT"])
                O("v", lambda v, tt=tt: v.tensor_copy(qkT[tt][:], PqkT[:]),
                  reads=["PqkT"], writes=[f"qkT{tt}"])

                def sc_fn(t, tt=tt):
                    for h in range(4):
                        ins = t.matmul(Psc[:, h * 128:(h + 1) * 128],
                                       qkT[tt][:, h * 128:(h + 1) * 128],
                                       qkT[tt][:, 512 + h * 128:512 + (h + 1) * 128],
                                       start=True, stop=True)
                    return ins
                O("t", sc_fn, reads=[f"qkT{tt}"], writes=["Psc"])
                O("v", lambda v: v.scalar_tensor_tensor(
                    smask[:].rearrange("p (h s) -> p h s", h=4),
                    Psc[:].rearrange("p (h s) -> p h s", h=4),
                    INV_SQRT_DK, mask_bc, ALU.mult, ALU.add),
                  reads=["Psc", "mask"], writes=["smask"])
                O("v", lambda v: v.reduce_max(
                    negmax[:], smask[:].rearrange("p (h s) -> p h s", h=4),
                    axis=AX.X, negate=True),
                  reads=["smask"], writes=["negmax"])
                for h in range(4):
                    O("s", lambda s, h=h: s.activation(
                        pall[:, h * 128:(h + 1) * 128], smask[:, h * 128:(h + 1) * 128],
                        AF.Exp, bias=negmax[:, h:h + 1], accum_out=rowsum[:, h:h + 1]),
                      reads=["smask", "negmax"], writes=["pall", "rowsum"], accum=True)
                O("v", lambda v: v.reciprocal(rsum[:], rowsum[:]),
                  reads=["rowsum"], writes=["rsum"])
                O("v", lambda v: v.tensor_tensor(
                    attnf[:].rearrange("p (h s) -> p h s", h=4),
                    pall[:].rearrange("p (h s) -> p h s", h=4),
                    rsum[:].unsqueeze(2).broadcast_to((128, 4, 128)), ALU.mult),
                  reads=["pall", "rsum"], writes=["attnf"])
                def aT_fn(t):
                    for h in range(4):
                        ins = t.transpose(Psc[:, h * 128:(h + 1) * 128],
                                          attnf[:, h * 128:(h + 1) * 128], eyef[:])
                    return ins
                O("t", aT_fn, reads=["attnf", "eyef"], writes=["Psc"])
                O("v", lambda v, tt=tt: v.tensor_copy(attnT[tt][:], Psc[:]),
                  reads=["Psc"], writes=[f"attnT{tt}"])

            def gprj(t):
                for dc in range(8):
                    ins = t.matmul(Pg[0:4, :], w_sb[:, dc, 1536:1540], xnf[:, dc, :],
                                   start=(dc == 0), stop=(dc == 7))
                return ins
            O("t", gprj, reads=["xnf0", "xnf1", "xnf2", "xnf3", "wall"], writes=["Pg"])
            O("s", lambda s: s.activation(ge[:], Pg[0:4, :], AF.Exp, scale=-1.0,
                                          bias=nbg[:]),
              reads=["Pg", "nbg"], writes=["ge"])
            O("s", lambda s: s.activation(glp[:], ge[:], AF.Ln, bias=ones1[0:4, :]),
              reads=["ge", "ones1"], writes=["glp"])
            O("v", lambda v: v.reduce_sum(
                lng[:], glp[:].rearrange("p (c s) -> p c s", c=8), axis=AX.X),
              reads=["glp"], writes=["lng"])
            O("s", lambda s: s.activation(gam[:], lng[:], AF.Exp, scale=-1.0),
              reads=["lng"], writes=["gam"])
            O("s", lambda s: s.copy(gambf[:], gam[:]), reads=["gam"], writes=["gambf"])
            O("y", lambda e: e.dma_start(out=gsc_d[:], in_=gambf[:]),
              reads=["gambf"], writes=["gdram"], dma=True, tag="gam")
            O("y", lambda e: e.dma_start(
                out=gamrow[:].rearrange("p (a b) -> p a b", a=4), in_=gsc_d[:]),
              reads=["gdram"], writes=["gamrow"], dma=True, tag="gam")
            O("t", lambda t: t.matmul(Pg[:, 0:32], onesrow[:], gamrow[:],
                                      start=True, stop=True),
              reads=["onesrow", "gamrow"], writes=["Pg"])
            O("v", lambda v: v.tensor_copy(gbc[:], Pg[:, 0:32]),
              reads=["Pg"], writes=["gbc"])

            for tt in range(4):
                ypar = (st * 4 + tt) % 2
                def yin_fn(t, sl=sl, tt=tt):
                    for h in range(4):
                        ins = t.matmul(Py[:, h * 128:(h + 1) * 128],
                                       v_sb[sl][:, tt, h * 128:(h + 1) * 128],
                                       attnT[tt][:, h * 128:(h + 1) * 128],
                                       start=True, stop=True)
                    return ins
                O("t", yin_fn, reads=[f"v{sl}_{tt}", f"attnT{tt}"], writes=["Py"])
                for j in range(2):
                    cs = tt * 2 + j
                    def ycr(t, tt=tt, j=j):
                        for h in range(4):
                            ins = t.matmul(
                                Pg[:, h * 128 + j * 64: h * 128 + j * 64 + 64],
                                S_b[:, h * 128:(h + 1) * 128],
                                qkT[tt][:, h * 128 + j * 64: h * 128 + j * 64 + 64],
                                start=True, stop=True)
                        return ins
                    O("t", ycr, reads=["Sb0", "Sb1", "Sb2", "Sb3", f"qkT{tt}"],
                      writes=["Pg"], accum=(j == 1))
                    pk = "Pq" if j == 0 else "Pk"
                    c0 = 0 if j == 0 else 512
                    def kv_fn(t, sl=sl, tt=tt, j=j, c0=c0):
                        for h in range(4):
                            ins = t.matmul(
                                Pqkv[:, c0 + h * 128: c0 + (h + 1) * 128],
                                kn[tt][j * 64:(j + 1) * 64, h * 128:(h + 1) * 128],
                                v_sb[sl][j * 64:(j + 1) * 64, tt, h * 128:(h + 1) * 128],
                                start=True, stop=True)
                        return ins
                    O("t", kv_fn, reads=[f"kn{tt}", f"v{sl}_{tt}"], writes=[pk])
                    for h in range(4):
                        O("v", lambda v, h=h, cs=cs, c0=c0: v.scalar_tensor_tensor(
                            S_f[:, h * 128:(h + 1) * 128], S_f[:, h * 128:(h + 1) * 128],
                            gbc[:, h * 8 + cs: h * 8 + cs + 1],
                            Pqkv[:, c0 + h * 128: c0 + (h + 1) * 128],
                            ALU.mult, ALU.add),
                          reads=[f"Sf{h}", "gbc", pk], writes=[f"Sf{h}"])
                        O("s", lambda s, h=h: s.copy(S_b[:, h * 128:(h + 1) * 128],
                                                     S_f[:, h * 128:(h + 1) * 128]),
                          reads=[f"Sf{h}"], writes=[f"Sb{h}"])
                O("s", lambda s: s.copy(attnf[:], Pg[:]),
                  reads=["Pg"], writes=["attnf"])
                O("v", lambda v, ypar=ypar: v.tensor_add(y_sb[ypar][:], Py[:], attnf[:]),
                  reads=["Py", "attnf"], writes=[f"ysb{ypar}"])
                O("y", lambda e, T0=T0, tt=tt, ypar=ypar: e.dma_start(
                    out=y_d[:, :, T0 + tt * 128: T0 + (tt + 1) * 128].transpose([1, 0, 2]),
                    in_=y_sb[ypar][:].rearrange("p (h t) -> p h t", h=4)),
                  reads=[f"ysb{ypar}"], dma=True, tag="yo")

        sems = {}
        with contextlib.ExitStack() as es2:
            for sk in p.used_streams():
                sems[sk] = es2.enter_context(nc.semaphore("sem_" + "_".join(map(str, sk))))
            with nc.Block() as block:
                p.emit(nc, block, sems)
    return nc


def build_phase2(TP=4096):
    nc = bass.Bass("TRN2", target_bir_lowering=False, debug=False)
    NST = TP // 256
    y_d = nc.dram_tensor("y", [8, 128, TP], BF16, kind="ExternalInput").ap()
    xr_d = nc.dram_tensor("xr", [TP, 1024], F32, kind="ExternalInput").ap()
    wo_d = nc.dram_tensor("wo", [1024, 1024], F32, kind="ExternalInput").ap()
    w1_d = nc.dram_tensor("w1", [1024, 4096], F32, kind="ExternalInput").ap()
    w2_d = nc.dram_tensor("w2", [4096, 1024], F32, kind="ExternalInput").ap()
    b1_d = nc.dram_tensor("b1", [128, 32], F32, kind="ExternalInput").ap()
    b2_d = nc.dram_tensor("b2", [1, 1024], F32, kind="ExternalInput").ap()
    out_d = nc.dram_tensor("out", [TP, 1024], F32, kind="ExternalOutput").ap()

    import contextlib
    es = contextlib.ExitStack()
    with es:
        wo_sb = _alloc(es, nc, "wo_sb", [128, 8, 1024], BF16)
        w1_sb = _alloc(es, nc, "w1_sb", [128, 8, 4096], BF16)
        w2_sb = _alloc(es, nc, "w2_sb", [128, 32, 1024], BF16)
        b1_sb = _alloc(es, nc, "b1_sb", [128, 32], F32)
        b2_sb = _alloc(es, nc, "b2_sb", [1, 1024], BF16)
        y_sb = [_alloc(es, nc, "y_sb0", [128, 8, 256], BF16)] * 2
        x_sb = [_alloc(es, nc, f"x_sb{i}", [128, 2, 1024], F32) for i in range(2)]
        x1 = [_alloc(es, nc, "x1_0", [128, 2, 1024], F32)] * 2
        xn2 = [_alloc(es, nc, "xn2_0", [128, 1024], BF16)] * 2
        xn2f = [_alloc(es, nc, f"xn2f{i}", [128, 8, 256], BF16) for i in range(2)]
        h1_sb = _alloc(es, nc, "h1_sb", [128, 32, 256], BF16)
        out_sb = [_alloc(es, nc, "out_sb0", [128, 1024], F32)] * 2
        scr = out_sb[0]
        stv = _alloc(es, nc, "stv", [128, 26], F32)
        onesrow = _alloc(es, nc, "onesrow", [1, 128], BF16)
        onesbf = _alloc(es, nc, "onesbf", [128, 128], BF16)
        eyeb = _alloc(es, nc, "eyeb", [128, 128], BF16)

        Po = es.enter_context(nc.psum_tensor("Po", [128, 1024], F32))
        PT = es.enter_context(nc.psum_tensor("PT", [128, 1024], BF16))
        Ph = [es.enter_context(nc.psum_tensor(f"Ph{i}", [128, 256], F32)) for i in range(2)]
        Pw = es.enter_context(nc.psum_tensor("Pw", [128, 1024], F32))

        p = Prog()
        O = p.op

        O("g", lambda g: g.dma_start(
            out=wo_sb[:], in_=wo_d.rearrange("(hc p) d -> p hc d", p=128)),
            writes=["wo"], dma=True, tag="ini")
        O("g", lambda g: g.dma_start(
            out=w1_sb[:], in_=w1_d.rearrange("(dc p) f -> p dc f", p=128)),
            writes=["w1"], dma=True, tag="ini")
        O("g", lambda g: g.dma_start(
            out=w2_sb[:], in_=w2_d.rearrange("(fc p) d -> p fc d", p=128)),
            writes=["w2"], dma=True, tag="ini")
        O("g", lambda g: g.dma_start(out=b2_sb[:], in_=b2_d[:]), writes=["b2"], dma=True, tag="ini")
        O("y", lambda e: e.dma_start(out=b1_sb[:], in_=b1_d[:]),
          writes=["b1"], dma=True, tag="ini")
        O("v", lambda v: v.memset(onesrow[:], 1.0), writes=["onesrow"])
        O("v", lambda v: v.memset(onesbf[:], 1.0), writes=["onesbf"])
        O("g", lambda g: g.affine_select(eyeb[:], onesbf[:], pattern=[[-1, 128]],
                                         compare_op=ALU.is_equal, fill=0.0,
                                         base=0, channel_multiplier=1),
          reads=["onesbf"], writes=["eyeb"])

        for st in range(NST):
            sl = st % 2
            T0 = st * 256
            O("y", lambda e, T0=T0, sl=sl: e.dma_start(
                out=x_sb[sl][:], in_=xr_d[T0:T0 + 256].rearrange("(tt p) d -> p tt d", p=128)),
                writes=[f"x{sl}"], dma=True, tag=f"x{sl}")
            O("y", lambda e, T0=T0, sl=sl: e.dma_start(
                out=y_sb[sl][:], in_=y_d[:, :, T0:T0 + 256].transpose([1, 0, 2])),
                writes=["y0"], dma=True, tag="yy")

            for tt in range(2):
                for ds in range(2):
                    def wo_fn(t, sl=sl, tt=tt, ds=ds):
                        for hc in range(8):
                            ins = t.matmul(Po[:, ds * 512:(ds + 1) * 512],
                                           y_sb[sl][:, hc, tt * 128:(tt + 1) * 128],
                                           wo_sb[:, hc, ds * 512:(ds + 1) * 512],
                                           start=(hc == 0), stop=(hc == 7))
                        return ins
                    O("t", wo_fn, reads=["y0", "wo"], writes=["Po"],
                      accum=(ds == 1))
                O("v", lambda v, sl=sl, tt=tt: v.tensor_add(
                    x1[sl][:, tt, :], Po[:], x_sb[sl][:, tt, :]),
                  reads=["Po", f"x{sl}"], writes=[f"x1_{sl}_{tt}"])
                O("v", lambda v, sl=sl, tt=tt: v.reduce_sum(
                    stv[:, tt:tt + 1], x1[sl][:, tt, :], axis=AX.X),
                  reads=[f"x1_{sl}_{tt}"], writes=["xsum"])
                O("s", lambda s, sl=sl, tt=tt: s.activation(
                    scr[:], x1[sl][:, tt, :], AF.Square, accum_out=stv[:, 2 + tt:3 + tt]),
                  reads=[f"x1_{sl}_{tt}"], writes=["out0", "xss"])
            O("v", lambda v: v.tensor_scalar_mul(stv[:, 4:6], stv[:, 0:2], 1.0 / 1024),
              reads=["xsum"], writes=["mean"])
            O("v", lambda v: v.tensor_scalar_mul(stv[:, 6:8], stv[:, 2:4], 1.0 / 1024),
              reads=["xss"], writes=["ms"])
            O("v", lambda v: v.tensor_tensor(stv[:, 8:10], stv[:, 4:6], stv[:, 4:6], ALU.mult),
              reads=["mean"], writes=["m2"])
            O("v", lambda v: v.tensor_sub(stv[:, 10:12], stv[:, 6:8], stv[:, 8:10]),
              reads=["ms", "m2"], writes=["var"])
            O("v", lambda v: v.tensor_scalar_add(stv[:, 12:14], stv[:, 10:12], 1e-5),
              reads=["var"], writes=["u"])
            O("v", lambda v: v.reciprocal(stv[:, 14:16], stv[:, 12:14]),
              reads=["u"], writes=["r"])
            O("v", lambda v: v.tensor_scalar(stv[:, 16:18], stv[:, 14:16], 1.0, 0.5,
                                             ALU.add, ALU.mult),
              reads=["r"], writes=["s_i"])
            for it in range(7):
                O("v", lambda v: v.reciprocal(stv[:, 18:20], stv[:, 16:18]),
                  reads=["s_i"], writes=["t_i"])
                O("v", lambda v: v.tensor_tensor(stv[:, 20:22], stv[:, 18:20],
                                                 stv[:, 14:16], ALU.mult),
                  reads=["t_i", "r"], writes=["w_i"])
                O("v", lambda v: v.tensor_add(stv[:, 22:24], stv[:, 20:22], stv[:, 16:18]),
                  reads=["w_i", "s_i"], writes=["z_i"])
                O("v", lambda v: v.tensor_scalar_mul(stv[:, 16:18], stv[:, 22:24], 0.5),
                  reads=["z_i"], writes=["s_i"])
            O("v", lambda v: v.scalar_tensor_tensor(stv[:, 24:26], stv[:, 4:6], -1.0,
                                                    stv[:, 16:18], ALU.mult, ALU.mult),
              reads=["mean", "s_i"], writes=["nmi"])

            for tt in range(2):
                O("s", lambda s, sl=sl, tt=tt: s.activation(
                    xn2[tt][:], x1[sl][:, tt, :], AF.Identity,
                    scale=stv[:, 16 + tt:17 + tt], bias=stv[:, 24 + tt:25 + tt]),
                  reads=[f"x1_{sl}_{tt}", "s_i", "nmi"], writes=["xn2k"])
                def xT_fn(t, tt=tt):
                    for k in range(8):
                        ins = t.transpose(PT[:, k * 128:(k + 1) * 128],
                                          xn2[tt][:, k * 128:(k + 1) * 128], eyeb[:])
                    return ins
                O("t", xT_fn, reads=["xn2k", "eyeb"], writes=["PT"])
                O("v", lambda v, sl=sl, tt=tt: v.tensor_copy(
                    xn2f[sl][:, :, tt * 128:(tt + 1) * 128],
                    PT[:].rearrange("p (dc t) -> p dc t", t=128)),
                  reads=["PT"], writes=[f"xn2f{sl}_{tt}"])

            for fc in range(32):
                pp = fc % 2
                def h1_fn(t, sl=sl, fc=fc, pp=pp):
                    for dc in range(8):
                        ins = t.matmul(Ph[pp][:], w1_sb[:, dc, fc * 128:(fc + 1) * 128],
                                       xn2f[sl][:, dc, :], start=(dc == 0), stop=(dc == 7))
                    return ins
                O("t", h1_fn, reads=[f"xn2f{sl}_0", f"xn2f{sl}_1", "w1"],
                  writes=[f"Ph{pp}"])
                O("s", lambda s, fc=fc, pp=pp: s.activation(
                    h1_sb[:, fc, :], Ph[pp][:], AF.Gelu, bias=b1_sb[:, fc:fc + 1]),
                  reads=[f"Ph{pp}", "b1"], writes=[f"h1_{fc}"])

            for tt in range(2):
                opar = (st * 2 + tt) % 2
                for ds in range(2):
                    def w2_fn(t, tt=tt, ds=ds):
                        for fc in range(32):
                            t.matmul(Pw[:, ds * 512:(ds + 1) * 512],
                                     h1_sb[:, fc, tt * 128:(tt + 1) * 128],
                                     w2_sb[:, fc, ds * 512:(ds + 1) * 512],
                                     start=(fc == 0), stop=False)
                        ins = t.matmul(Pw[:, ds * 512:(ds + 1) * 512], onesrow[:],
                                       b2_sb[:, ds * 512:(ds + 1) * 512],
                                       start=False, stop=True)
                        return ins
                    O("t", w2_fn,
                      reads=[f"h1_{fc}" for fc in range(32)] + ["w2", "onesrow", "b2"],
                      writes=["Pw"], accum=(ds == 1))
                O("v", lambda v, sl=sl, tt=tt, opar=opar: v.tensor_add(
                    out_sb[opar][:], Pw[:], x1[sl][:, tt, :]),
                  reads=["Pw", f"x1_{sl}_{tt}"], writes=["out0"])
                O("y", lambda e, T0=T0, tt=tt, opar=opar: e.dma_start(
                    out=out_d[T0 + tt * 128: T0 + (tt + 1) * 128, :], in_=out_sb[opar][:]),
                  reads=["out0"], dma=True, tag="out")

        sems = {}
        import contextlib as _cl
        with _cl.ExitStack() as es2:
            for sk in p.used_streams():
                sems[sk] = es2.enter_context(nc.semaphore("sem_" + "_".join(map(str, sk))))
            with nc.Block() as block:
                p.emit(nc, block, sems)
    return nc


def kernel(L0_attn_x, L0_attn_Wq, L0_attn_Wk, L0_attn_Wv, L0_attn_Wg, L0_attn_bg,
           L0_attn_Wo, L0_attn_qk_gain, L0_ffn_w1, L0_ffn_b1, L0_ffn_w2, L0_ffn_b2):
    x = np.ascontiguousarray(np.asarray(L0_attn_x, dtype=np.float32))
    WqT = np.ascontiguousarray(np.asarray(L0_attn_Wq, np.float32).T)
    WkT = np.ascontiguousarray(np.asarray(L0_attn_Wk, np.float32).T)
    WvT = np.ascontiguousarray(np.asarray(L0_attn_Wv, np.float32).T)
    WgT = np.ascontiguousarray(np.asarray(L0_attn_Wg, np.float32).T)
    WoT = np.ascontiguousarray(np.asarray(L0_attn_Wo, np.float32).T)
    bg = np.asarray(L0_attn_bg, np.float32)
    gain = np.ascontiguousarray(
        np.broadcast_to(np.asarray(L0_attn_qk_gain, np.float32).reshape(1, 1), (128, 1)))
    w1 = np.ascontiguousarray(np.asarray(L0_ffn_w1, np.float32))
    b1 = np.ascontiguousarray(np.asarray(L0_ffn_b1, np.float32))
    w2 = np.ascontiguousarray(np.asarray(L0_ffn_w2, np.float32))
    b2 = np.ascontiguousarray(np.asarray(L0_ffn_b2, np.float32).reshape(1, 1024))
    b1t = np.ascontiguousarray(b1.reshape(32, 128).T)

    wqkvg = [np.ascontiguousarray(np.concatenate(
        [WqT[:, hg * 512:(hg + 1) * 512], WkT[:, hg * 512:(hg + 1) * 512],
         WvT[:, hg * 512:(hg + 1) * 512], WgT[:, hg * 4:(hg + 1) * 4]], axis=1))
        for hg in range(2)]

    nc1 = build_phase1()
    in1 = []
    for c in range(8):
        b, hg = c // 2, c % 2
        in1.append({"x": x[b], "wqkvg": wqkvg[hg],
                    "bg": np.ascontiguousarray(bg[hg * 4:(hg + 1) * 4].reshape(4, 1)),
                    "gain": gain})
    res1 = run_bass_kernel_spmd(nc1, in1, core_ids=list(range(8)))
    ys = [np.asarray(res1.results[c]["y"]) for c in range(8)]

    nc2 = build_phase2()
    in2 = []
    for c in range(8):
        b, th = c // 2, c % 2
        yfull = np.ascontiguousarray(
            np.concatenate([ys[2 * b], ys[2 * b + 1]], axis=0)[:, :, th * 4096:(th + 1) * 4096])
        in2.append({"y": yfull, "xr": np.ascontiguousarray(x[b, th * 4096:(th + 1) * 4096]),
                    "wo": WoT, "w1": w1, "w2": w2, "b1": b1t, "b2": b2})
    res2 = run_bass_kernel_spmd(nc2, in2, core_ids=list(range(8)))

    out = np.empty((4, 8192, 1024), np.float32)
    for c in range(8):
        b, th = c // 2, c % 2
        out[b, th * 4096:(th + 1) * 4096] = np.asarray(res2.results[c]["out"])
    return out
